# version 07 (final)
# baseline (speedup 1.0000x reference)
import contextlib
import numpy as np
import ml_dtypes
import concourse.bass as bass
import concourse.mybir as mybir
from concourse.bass_utils import run_bass_kernel_spmd

F32 = mybir.dt.float32
BF16 = mybir.dt.bfloat16
AF = mybir.ActivationFunctionType
ALU = mybir.AluOpType
AX = mybir.AxisListType
MAGIC = 12582912.0
BIG = float(2 ** 15)
NCORE = 8
CT = 2048
ST = 2056
NT1 = 17
T2 = 256

def build_nc():
    nc = bass.Bass("TRN2", target_bir_lowering=False)
    dram = {}
    def din(name, shape, dt):
        dram[name] = nc.dram_tensor(name, shape, dt, kind="ExternalInput").ap()
        return dram[name]
    xfm_h = din("xfm", [1024, ST], F32)
    xtm_h = din("xtm", [ST, 1024], F32)
    wq_h = din("wq", [1024, 1024], F32)
    wdn_h = din("wdn", [1024, 256], F32)
    wuk_h = din("wuk", [256, 1024], F32)
    wuv_h = din("wuv", [256, 1024], F32)
    wo_h = din("wo", [1024, 1024], F32)
    w1t_h = din("w1t", [1024, 4096], F32)
    w2t_h = din("w2t", [4096, 1024], F32)
    bq_h = din("bq", [1, 1024], F32)
    bdn_h = din("bdn", [1, 256], F32)
    buk_h = din("buk", [1, 1024], F32)
    buv_h = din("buv", [1, 1024], F32)
    bo_h = din("bo", [1, 1024], F32)
    cw_h = din("cw", [128, 128], F32)
    hm_h = din("hm", [128, NT1], F32)
    mq_h = din("mq", [8, 2048], BF16)
    mk_h = din("mk", [8, 2048], BF16)
    idb_h = din("idb", [128, 128], BF16)
    idf_h = din("idf", [128, 128], F32)
    onesb_h = din("onesb", [1, 512], BF16)
    onesf_h = din("onesf", [1, 128], F32)
    onescf_h = din("onescf", [128, 1], F32)
    out_h = nc.dram_tensor("out", [CT, 1024], F32, kind="ExternalOutput").ap()
    vtrip_h = nc.dram_tensor("vtrip", [16, ST, 64], BF16).ap()
    atrip_h = nc.dram_tensor("atrip", [ST, 1024], BF16).ap()
    xqt_h = nc.dram_tensor("xqt", [8, 128, ST], BF16).ap()
    y1s_h = nc.dram_tensor("y1s", [ST, 1024], F32).ap()
    hs1_h = nc.dram_tensor("hs1", [ST, 1], F32).ap()
    h3d_h = nc.dram_tensor("h3d", [4096, 256], F32).ap()

    cx = contextlib.ExitStack()
    cx1 = contextlib.ExitStack()
    def sb(n, sh, d): return cx.enter_context(nc.sbuf_tensor(n, sh, d))
    def sb1(n, sh, d): return cx1.enter_context(nc.sbuf_tensor(n, sh, d))
    def ps(n, sh, d): return cx.enter_context(nc.psum_tensor(n, sh, d))

    p0 = ps("p0", [128, 512], F32)
    p_att = ps("p_att", [128, 512], F32)
    p_av = ps("p_av", [128, 260], F32)
    p_x = ps("p_x", [128, 1024], BF16)
    p_h = ps("p_h", [128, 258], F32)
    p_o = ps("p_o", [128, 256], F32)
    p_s = ps("p_s", [128, 512], F32)
    cw = sb("s_cw", [128, 128], F32)
    hm = sb("s_hm", [128, NT1], F32)
    idb = sb("s_idb", [128, 128], BF16)
    idf = sb("s_idf", [128, 128], F32)
    onesb = sb("s_onesb", [1, 512], BF16)
    onesf = sb("s_onesf", [1, 128], F32)
    onescf = sb("s_onescf", [128, 1], F32)
    wst = sb("wst", [128, 4096], F32)
    res = sb("res", [128, 1024], F32)
    stats = sb("stats", [128, 12], F32)
    mv = sb("mv", [128, 2], F32)
    sc1 = sb("sc1", [128, 1], F32)
    sc2 = sb("sc2", [128, 1], F32)
    sc3 = sb("sc3", [128, 1], F32)
    y1 = sb("y1", [128, 1024], F32)
    xqw = sb("xqw", [128, 2064], BF16)
    y1b = sb("y1b", [128, 1024], F32)
    t1sb = sb("t1sb", [128, 258], F32)
    t2sb = sb("t2sb", [128, 256], F32)
    scr = sb("scr", [128, 256], F32)
    am = sb("am", [128, 256], F32)
    amtok = sb("amtok", [128, 64], F32)
    amt = sb("amt", [128, 2], F32)
    qsc = sb("qsc", [128, 2], F32)
    osc = sb("osc", [128, 2], F32)
    qsrow = sb("qsrow", [1, 256], F32)
    osrow = sb("osrow", [1, 256], F32)
    qsb2 = sb("qsb2", [128, 256], F32)
    osb2 = sb("osb2", [128, 256], F32)
    ffn = sb("ffn", [128, 2048], BF16)
    ftm = sb("ftm", [128, 1024], BF16)
    ftm32 = sb("ftm32", [128, 1024], F32)
    prt = sb("prt", [128, 16], F32)
    sval = sb("sval", [1, 4], F32)
    m2b = sb("m2b", [128, 1], F32)
    m1b = sb("m1b", [128, 1], F32)
    hsv = sb("hsv", [128, 1], F32)
    hsrow = sb("hsrow", [1, 258], F32)
    hsb = sb("hsb", [128, 258], F32)
    hr = sb("hr", [128, 258], F32)
    s1b = sb("s1b", [128, 1], F32)
    s2b = sb("s2b", [128, 1], F32)
    epsb = sb("epsb", [128, 1], F32)
    wqb = sb1("wqb", [128, 8192], BF16)
    wdnb = sb1("wdnb", [128, 2048], BF16)
    wukb = sb1("wukb", [128, 2048], BF16)
    wuvb = sb1("wuvb", [128, 2048], BF16)
    wob = sb1("wob", [128, 8192], BF16)
    bqb = sb1("bqb", [1, 1024], BF16)
    bdnb = sb1("bdnb", [1, 256], BF16)
    bukb = sb1("bukb", [1, 1024], BF16)
    buvb = sb1("buvb", [1, 1024], BF16)
    bob = sb1("bob", [1, 1024], BF16)
    xfm32 = sb1("xfm32", [128, 1024], F32)
    xfmb = sb1("xfmb", [128, 1024], BF16)
    xtm32 = sb1("xtm32", [128, 1024], F32)
    qdm = sb1("qdm", [72, 2048], BF16)
    kdm = sb1("kdm", [72, 2048], BF16)
    latb = sb1("latb", [128, 256], BF16)
    vtok = sb1("vtok", [128, 1024], BF16)
    vg = sb1("vg", [128, 260], BF16)
    attm = sb1("attm", [128, 512], BF16)
    sg = sb1("sg", [128, 4], F32)
    srec = sb1("srec", [128, 4], F32)
    attn_n = sb1("attn_n", [128, 1024], BF16)
    atm = sb1("atm", [128, 1024], BF16)
    atT = sb1("atT", [128, 1024], BF16)
    tq = sb1("tq", [128, 1024], F32)
    xq = sb1("xq", [128, 1024], BF16)
    xqs = sb1("xqs", [128, 1024], BF16)

    sems = {n: cx.enter_context(nc.semaphore(n)) for n in
            ["dI", "dO", "sT", "sV", "sS"]}
    cnt = {n: 0 for n in sems}
    prog = []
    last = [None, 0]

    def phase(eng, semname, fn, n=1):
        waits = [(last[0], last[1])] if last[0] is not None else []
        cnt[semname] += n
        prog.append((eng, waits, fn, semname, n))
        last[0], last[1] = semname, cnt[semname]

    def cphase(eng, semname, fn):
        def wrap(e, _fn=fn, _s=semname):
            _fn(e)
            e.drain().then_inc(sems[_s], 1)
        phase(eng, semname, wrap, 1)

    def dphase(eng, semname, dmalist):
        def wrap(e, _dl=dmalist, _s=semname):
            for (o, i) in _dl:
                e.dma_start(out=o, in_=i).then_inc(sems[_s], 16)
        phase(eng, semname, wrap, 16 * len(dmalist))

    MM = lambda e, out, l, r, st_, sp: e.matmul(out, l, r, start=st_, stop=sp)

    dphase("sync", "dI", [(cw[:], cw_h[:]), (hm[:], hm_h[:]),
                          (idb[:], idb_h[:]), (idf[:], idf_h[:]),
                          (onesb[:], onesb_h[:]), (onesf[:], onesf_h[:]),
                          (onescf[:], onescf_h[:]),
                          (qdm[64:72, :], mq_h[:]), (kdm[64:72, :], mk_h[:])])
    for (wb, wh, nch, w) in [(wqb, wq_h, 8, 1024), (wdnb, wdn_h, 8, 256),
                             (wukb, wuk_h, 2, 1024), (wuvb, wuv_h, 2, 1024),
                             (wob, wo_h, 8, 1024)]:
        for k in range(nch):
            dphase("sync", "dI", [(wst[:, 0:w], wh[128 * k:128 * k + 128, :])])
            cphase("vector", "sV", (lambda e, _wb=wb, _k=k, _w=w:
                e.tensor_copy(_wb[:, _k * _w:(_k + 1) * _w], wst[:, 0:_w])))
    for (bb, bh, w) in [(bqb, bq_h, 1024), (bdnb, bdn_h, 256), (bukb, buk_h, 1024),
                        (buvb, buv_h, 1024), (bob, bo_h, 1024)]:
        dphase("sync", "dI", [(wst[0:1, 0:w], bh[:])])
        cphase("vector", "sV", (lambda e, _b=bb, _w=w:
            e.tensor_copy(_b[:, 0:_w], wst[0:1, 0:_w])))
    cphase("vector", "sV", lambda e: (e.memset(vg[:, 64::65], 1.0), e.drain(),
                                       e.memset(epsb[:], 1e-5)))

    def absmean(wh, nch, w, col0, nelts):
        for k in range(nch):
            dphase("sync", "dI", [(wst[:, 0:w], wh[128 * k:128 * k + 128, :])])
            cphase("vector", "sV", (lambda e, _k=k, _w=w, _c=col0:
                e.reduce_sum(prt[:, _c + _k:_c + _k + 1], wst[:, 0:_w],
                             axis=AX.X, apply_absolute_value=True)))
        def fin(e, _c=col0, _n=nch):
            e.reduce_sum(sc1[:], prt[:, _c:_c + _n], axis=AX.X)
        cphase("vector", "sV", fin)
        cphase("tensor", "sT", lambda e: MM(e, p_s[0:1, 0:1], sc1[:], onescf[:], True, True))
    absmean(w1t_h, 8, 4096, 0, 4096 * 1024)
    def w1fin(e):
        e.tensor_single_scalar(sval[0:1, 0:1], p_s[0:1, 0:1], 1.0 / (4096.0 * 1024.0), ALU.mult)
        e.drain()
        e.tensor_single_scalar(sval[0:1, 0:1], sval[0:1, 0:1], 1e-5, ALU.max)
        e.drain()
        e.reciprocal(sval[0:1, 1:2], sval[0:1, 0:1])
    cphase("vector", "sV", w1fin)
    cphase("tensor", "sT", lambda e: (MM(e, p_s[:, 0:1], onesf[:], sval[0:1, 1:2], True, True),
                                      MM(e, p_s[:, 1:2], onesf[:], sval[0:1, 0:1], True, True)))
    cphase("vector", "sV", lambda e: (e.tensor_copy(s1b[:], p_s[:, 0:1]), e.drain(),
                                      e.tensor_copy(m1b[:], p_s[:, 1:2])))
    for half in range(2):
        for k in range(16):
            kk = half * 16 + k
            dphase("sync", "dI", [(wst[:, 0:1024], w2t_h[128 * kk:128 * kk + 128, :])])
            cphase("vector", "sV", (lambda e, _k=k:
                e.reduce_sum(prt[:, _k:_k + 1], wst[:, 0:1024],
                             axis=AX.X, apply_absolute_value=True)))
        cphase("vector", "sV", (lambda e: e.reduce_sum(sc1[:], prt[:, 0:16], axis=AX.X)))
        cphase("tensor", "sT", (lambda e, _h=half:
            MM(e, p_s[0:1, 2:3], sc1[:], onescf[:], _h == 0, _h == 1)))
    def w2fin(e):
        e.tensor_single_scalar(sval[0:1, 2:3], p_s[0:1, 2:3], 1.0 / (4096.0 * 1024.0), ALU.mult)
        e.drain()
        e.tensor_single_scalar(sval[0:1, 2:3], sval[0:1, 2:3], 1e-5, ALU.max)
        e.drain()
        e.reciprocal(sval[0:1, 3:4], sval[0:1, 2:3])
    cphase("vector", "sV", w2fin)
    cphase("tensor", "sT", lambda e: (MM(e, p_s[:, 2:3], onesf[:], sval[0:1, 3:4], True, True),
                                      MM(e, p_s[:, 3:4], onesf[:], sval[0:1, 2:3], True, True)))
    cphase("vector", "sV", lambda e: (e.tensor_copy(s2b[:], p_s[:, 2:3]),
                                      e.drain(), e.tensor_copy(m2b[:], p_s[:, 3:4])))

    for it in range(NT1):
        c0 = it * 128
        tw = 128 if it < 16 else 8
        ngb = tw // 8
        dphase("sync", "dI", [
            (xfm32[:, 0:8 * tw],
             bass.AP(tensor=xfm_h.tensor, offset=c0,
                     ap=[[ST, 128], [128 * ST, 8], [1, tw]])),
            (xtm32[0:tw, :], xtm_h[c0:c0 + tw, :])])
        cphase("vector", "sV", (lambda e, _tw=tw:
            e.tensor_copy(xfmb[:, 0:8 * _tw], xfm32[:, 0:8 * _tw])))

        def project(wb, bb, nk, oc, tw):
            def f(e, _wb=wb, _bb=bb, _nk=nk, _oc=oc, _tw=tw):
                w = 1024 if _wb is not wdnb else 256
                for k in range(_nk):
                    MM(e, p0[:, 0:_tw], _wb[:, k * w + _oc * 128:k * w + _oc * 128 + 128],
                       xfmb[:, k * _tw:(k + 1) * _tw] if _wb in (wqb, wdnb)
                       else latb[:, k * _tw:(k + 1) * _tw], True if k == 0 else False, False)
                MM(e, p0[:, 0:_tw], _bb[:, _oc * 128:_oc * 128 + 128],
                   onesb[:, 0:_tw], False, True)
            return f

        for oc in range(8):
            cphase("tensor", "sT", project(wqb, bqb, 8, oc, tw))
            def ev(e, _oc=oc, _tw=tw, _ng=ngb):
                srcv = p0[:, 0:_tw].rearrange("p (g b) -> p g b", b=8)
                dv = qdm[0:64, :].rearrange("p (g h b) -> p g h b", h=16, b=8)
                e.tensor_copy(dv[:, 0:_ng, 2 * _oc, :], srcv[0:64])
                e.drain()
                dv2 = qdm[0:64, :].rearrange("p (g h b) -> p g h b", h=16, b=8)
                s2 = bass.AP(tensor=p0.ap().tensor, offset=p0.ap().offset,
                             ap=[[1, 64]] + [list(x) for x in srcv.ap[1:]])
                e.tensor_copy(dv2[:, 0:_ng, 2 * _oc + 1, :], p0[64:128, 0:_tw].rearrange("p (g b) -> p g b", b=8))
            cphase("vector", "sV", ev)
        for oc in range(2):
            cphase("tensor", "sT", project(wdnb, bdnb, 8, oc, tw))
            cphase("vector", "sV", (lambda e, _oc=oc, _tw=tw:
                e.tensor_copy(latb[:, _oc * _tw:(_oc + 1) * _tw], p0[:, 0:_tw])))
        for oc in range(8):
            cphase("tensor", "sT", project(wukb, bukb, 2, oc, tw))
            def evk(e, _oc=oc, _tw=tw, _ng=ngb):
                dv = kdm[0:64, :].rearrange("p (g b h) -> p g b h", b=8, h=16)
                e.tensor_copy(dv[:, 0:_ng, :, 2 * _oc],
                              p0[0:64, 0:_tw].rearrange("p (g b) -> p g b", b=8))
                e.drain()
                e.tensor_copy(dv[:, 0:_ng, :, 2 * _oc + 1],
                              p0[64:128, 0:_tw].rearrange("p (g b) -> p g b", b=8))
            cphase("vector", "sV", evk)
        for half in range(2):
            def fv(e, _h=half, _tw=tw):
                for lk in range(2):
                    MM(e, p0[0:_tw, 0:512], latb[:, lk * _tw:(lk + 1) * _tw],
                       wuvb[:, lk * 1024 + _h * 512:lk * 1024 + _h * 512 + 512],
                       lk == 0, False)
                MM(e, p0[0:_tw, 0:512], onesb[0:1, 0:_tw],
                   buvb[:, _h * 512:_h * 512 + 512], False, True)
            cphase("tensor", "sT", fv)
            cphase("vector", "sV", (lambda e, _h=half, _tw=tw:
                e.tensor_copy(vtok[0:_tw, _h * 512:_h * 512 + 512], p0[0:_tw, 0:512])))
        dphase("gpsimd", "dO", [(
            vtrip_h.rearrange("g t d -> t g d")[c0:c0 + tw, :, :],
            vtok[0:tw, :].rearrange("p (g d) -> p g d", g=16))])
        nbank = (tw + 31) // 32
        for nb in range(nbank):
            ngi = min(4, ngb - nb * 4)
            dphase("sync", "dI", [(
                vg[tb * 16:tb * 16 + 16, :].rearrange("p (g e) -> p g e", e=65)[:, 0:ngi, 0:64],
                bass.AP(tensor=vtrip_h.tensor,
                        offset=(c0 + nb * 32 + tb) * 64,
                        ap=[[ST * 64, 16], [512, ngi], [1, 64]]))
                for tb in range(8)])
            def fqk(e, _nb=nb, _ngi=ngi):
                for gi in range(_ngi):
                    g = _nb * 4 + gi
                    MM(e, p_att[:, gi * 128:gi * 128 + 128],
                       kdm[:, g * 128:g * 128 + 128],
                       qdm[:, g * 128:g * 128 + 128], True, True)
            cphase("tensor", "sT", fqk)
            cphase("scalar", "sS", (lambda e, _n=ngi:
                e.activation(attm[:, 0:_n * 128], p_att[:, 0:_n * 128],
                             AF.Exp, scale=0.125)))
            def fav(e, _ngi=ngi):
                for gi in range(_ngi):
                    MM(e, p_av[:, gi * 65:gi * 65 + 65],
                       attm[:, gi * 128:gi * 128 + 128],
                       vg[:, gi * 65:gi * 65 + 65], True, True)
            cphase("tensor", "sT", fav)
            def fnorm(e, _nb=nb, _ngi=ngi):
                e.tensor_copy(sg[:, 0:_ngi], p_av[:, 64:65 * _ngi:65])
                e.drain()
                e.reciprocal(srec[:, 0:_ngi], sg[:, 0:_ngi])
                e.drain()
                sbc = bass.AP(tensor=srec.ap().tensor, offset=srec.ap().offset,
                              ap=[list(srec.ap().ap[0]), [1, _ngi], [0, 64]])
                dc = p_av[:].rearrange("p (g e) -> p g e", e=65)[:, 0:_ngi, 0:64]
                dst = attn_n[:, _nb * 256:_nb * 256 + _ngi * 64].rearrange(
                    "p (g d) -> p g d", d=64)
                e.tensor_tensor(dst, dc, sbc, ALU.mult)
            cphase("vector", "sV", fnorm)
        dphase("gpsimd", "dO", [(
            atrip_h[c0:c0 + tw, :].rearrange("(g b) (h d) -> h b g d", b=8, h=16)[h],
            attn_n[h * 8:h * 8 + 8, 0:ngb * 64]) for h in range(16)])
        dphase("sync", "dI", [(atm[0:tw, :], atrip_h[c0:c0 + tw, :])])
        def ftr(e, _tw=tw):
            for k in range(8):
                e.transpose(p_x[:, k * _tw:(k + 1) * _tw],
                            atm[0:_tw, k * 128:(k + 1) * 128], idb[0:_tw, 0:_tw])
        cphase("tensor", "sT", ftr)
        cphase("vector", "sV", (lambda e, _tw=tw:
            e.tensor_copy(atT[:, 0:8 * _tw], p_x[:, 0:8 * _tw])))
        for half in range(2):
            def fwo(e, _h=half, _tw=tw):
                for k in range(8):
                    MM(e, p0[0:_tw, 0:512], atT[:, k * _tw:(k + 1) * _tw],
                       wob[:, k * 1024 + _h * 512:k * 1024 + _h * 512 + 512],
                       k == 0, False)
                MM(e, p0[0:_tw, 0:512], onesb[0:1, 0:_tw],
                   bob[:, _h * 512:_h * 512 + 512], False, True)
            cphase("tensor", "sT", fwo)
            cphase("vector", "sV", (lambda e, _h=half, _tw=tw:
                e.tensor_add(res[0:_tw, _h * 512:_h * 512 + 512],
                             p0[0:_tw, 0:512], xtm32[0:_tw, _h * 512:_h * 512 + 512])))
        def fln(e, _tw=tw):
            e.bn_stats(stats[0:_tw, 0:6], res[0:_tw, 0:512])
            e.bn_stats(stats[0:_tw, 6:12], res[0:_tw, 512:1024])
            e.drain()
            e.bn_aggr(mv[0:_tw, :], stats[0:_tw, :])
        cphase("vector", "sV", fln)
        cphase("scalar", "sS", (lambda e, _tw=tw:
            e.activation(sc1[0:_tw, :], mv[0:_tw, 1:2], AF.Sqrt, bias=epsb[0:_tw, :])))
        def fln2(e, _tw=tw):
            e.reciprocal(sc2[0:_tw, :], sc1[0:_tw, :])
            e.drain()
            e.tensor_single_scalar(sc3[0:_tw, :], mv[0:_tw, 0:1], -1.0, ALU.mult)
            e.drain()
            e.tensor_scalar(y1[0:_tw, :], res[0:_tw, :], sc3[0:_tw, :],
                            sc2[0:_tw, :], ALU.add, ALU.mult)
        cphase("vector", "sV", fln2)
        dphase("gpsimd", "dO", [(y1s_h[c0:c0 + tw, :], y1[0:tw, :])])
        def fq1(e, _tw=tw, _it=it):
            e.reduce_max(sc1[0:_tw, :], y1[0:_tw, :], axis=AX.X,
                         apply_absolute_value=True)
            e.drain()
            e.tensor_single_scalar(sc1[0:_tw, :], sc1[0:_tw, :], 1e-5, ALU.max)
            e.drain()
            e.reciprocal(sc2[0:_tw, :], sc1[0:_tw, :])
            e.drain()
            e.tensor_single_scalar(sc2[0:_tw, :], sc2[0:_tw, :], 127.0, ALU.mult)
            e.drain()
            e.tensor_scalar(tq[0:_tw, :], y1[0:_tw, :], sc2[0:_tw, :], MAGIC,
                            ALU.mult, ALU.add)
            e.drain()
            e.tensor_scalar(xq[0:_tw, :], tq[0:_tw, :], MAGIC,
                            hm[0:_tw, _it:_it + 1], ALU.subtract, ALU.mult)
            e.drain()
            e.tensor_scalar(hsv[0:_tw, :], sc1[0:_tw, :], m1b[0:_tw, :],
                            1.0 / 127.0, ALU.mult, ALU.mult)
        cphase("vector", "sV", fq1)
        dphase("gpsimd", "dO", [(hs1_h[c0:c0 + tw, :], hsv[0:tw, :])])
        def ftx(e, _tw=tw):
            for k in range(8):
                e.transpose(p_x[:, k * _tw:(k + 1) * _tw],
                            xq[0:_tw, k * 128:(k + 1) * 128], idb[0:_tw, 0:_tw])
        cphase("tensor", "sT", ftx)
        cphase("vector", "sV", (lambda e, _tw=tw:
            e.tensor_copy(xqs[:, 0:8 * _tw], p_x[:, 0:8 * _tw])))
        dphase("gpsimd", "dO", [(
            xqt_h.rearrange("k d t -> d k t")[:, :, c0:c0 + tw],
            xqs[:, 0:8 * tw].rearrange("p (k t) -> p k t", k=8))])

    cx1.close()
    w1qb = sb("w1qb", [128, 32768], BF16)
    w2qb = sb("w2qb", [128, 32768], BF16)
    k3 = sb("k3", [128, 8192], BF16)
    h3s = sb("h3s", [128, 256], F32)
    amx = sb("amx", [128, 256], F32)
    amn = sb("amn", [128, 256], F32)
    def quantw(wh, wb, nch, w, sbv):
        for k in range(nch):
            dphase("sync", "dI", [(wst[:, 0:w], wh[128 * k:128 * k + 128, :])])
            def fq(e, _k=k, _w=w, _s=sbv, _wb=wb):
                e.tensor_scalar(wst[:, 0:_w], wst[:, 0:_w], _s[:], MAGIC,
                                ALU.mult, ALU.add)
                e.drain()
                e.tensor_scalar(wst[:, 0:_w], wst[:, 0:_w], MAGIC, 1.0,
                                ALU.subtract, ALU.min)
                e.drain()
                e.tensor_scalar(_wb[:, _k * _w:(_k + 1) * _w], wst[:, 0:_w],
                                -1.0, None, ALU.max)
            cphase("vector", "sV", fq)
    quantw(w1t_h, w1qb, 8, 4096, s1b)
    quantw(w2t_h, w2qb, 32, 1024, s2b)

    for j in range(8):
        w0 = j * 256
        dphase("sync", "dI", [(
            xqw[:, 0:8 * 258],
            bass.AP(tensor=xqt_h.tensor, offset=w0,
                    ap=[[ST, 128], [128 * ST, 8], [1, 258]])),
            (hsrow[0:1, 0:258], hs1_h[w0:w0 + 258, :].rearrange("t o -> o t"))])
        cphase("tensor", "sT", lambda e: MM(e, p_h[:], onesf[:], hsrow[:], True, True))
        cphase("vector", "sV", lambda e: e.tensor_copy(hsb[:], p_h[:]))
        cphase("vector", "sV", lambda e: (e.memset(amx[:], -1e30), e.drain(),
                                          e.memset(amn[:], 1e30)))
        for s in range(32):
            def fw1(e, _s=s):
                for k in range(8):
                    MM(e, p_h[:], w1qb[:, k * 4096 + _s * 128:k * 4096 + _s * 128 + 128],
                       xqw[:, k * 258:(k + 1) * 258], k == 0, k == 7)
            cphase("tensor", "sT", fw1)
            cphase("vector", "sV", (lambda e:
                e.tensor_tensor(hr[:], p_h[:], hsb[:], ALU.mult)))
            cphase("scalar", "sS", (lambda e, _s=s:
                e.activation(t1sb[:], hr[:], AF.Identity, scale=cw[:, 32 + _s:33 + _s])))
            def fcv(e, _s=s):
                e.scalar_tensor_tensor(t2sb[:], hr[:, 0:256], cw[:, _s:_s + 1],
                                       t1sb[:, 1:257], ALU.mult, ALU.add)
                e.drain()
                e.scalar_tensor_tensor(scr[:], hr[:, 2:258], cw[:, 64 + _s:65 + _s],
                                       t2sb[:], ALU.mult, ALU.add)
            cphase("vector", "sV", fcv)
            cphase("scalar", "sS", (lambda e, _s=s:
                e.activation(h3s[:], scr[:], AF.Silu,
                             bias=cw[:, 96 + _s:97 + _s])))
            dphase("gpsimd", "dO", [(h3d_h[s * 128:(s + 1) * 128, :], h3s[:])])
            cphase("vector", "sV", lambda e: (
                e.tensor_max(amx[:], amx[:], h3s[:]), e.drain(),
                e.tensor_tensor(amn[:], amn[:], h3s[:], ALU.min)))
        def fam(e):
            e.tensor_single_scalar(am[:], amn[:], -1.0, ALU.mult)
            e.drain()
            e.tensor_max(am[:], am[:], amx[:])
            e.drain()
            e.tensor_copy(scr[0:64, :], am[64:128, :])
            e.drain()
            e.tensor_max(am[0:64, :], am[0:64, :], scr[0:64, :])
            e.drain()
            e.tensor_copy(scr[0:32, :], am[32:64, :])
            e.drain()
            e.tensor_max(am[0:32, :], am[0:32, :], scr[0:32, :])
        cphase("vector", "sV", fam)
        cphase("tensor", "sT", lambda e: (
            e.transpose(p_s[:, 0:32], am[0:32, 0:128], idf[0:32, 0:32]),
            e.transpose(p_s[:, 32:64], am[0:32, 128:256], idf[0:32, 0:32])))
        def fsc(e):
            e.tensor_copy(amtok[:], p_s[:, 0:64])
            e.drain()
            e.reduce_max(amt[:, 0:1], amtok[:, 0:32], axis=AX.X)
            e.reduce_max(amt[:, 1:2], amtok[:, 32:64], axis=AX.X)
            e.drain()
            e.tensor_single_scalar(amt[:], amt[:], 1e-5, ALU.max)
            e.drain()
            e.reciprocal(qsc[:], amt[:])
            e.drain()
            e.tensor_single_scalar(qsc[:], qsc[:], 127.0, ALU.mult)
            e.tensor_scalar(osc[:], amt[:], m2b[:], 1.0 / 127.0, ALU.mult, ALU.mult)
        cphase("vector", "sV", fsc)
        cphase("tensor", "sT", lambda e: (
            e.transpose(p_s[0:1, 0:128], qsc[:, 0:1], idf[:]),
            e.transpose(p_s[0:1, 128:256], qsc[:, 1:2], idf[:]),
            e.transpose(p_s[0:1, 256:384], osc[:, 0:1], idf[:]),
            e.transpose(p_s[0:1, 384:512], osc[:, 1:2], idf[:])))
        def fsr(e):
            e.tensor_copy(qsrow[0:1, 0:256], p_s[0:1, 0:256])
            e.tensor_copy(osrow[0:1, 0:256], p_s[0:1, 256:512])
        cphase("vector", "sV", fsr)
        cphase("tensor", "sT", lambda e: (
            MM(e, p_o[:], onesf[:], qsrow[:], True, True),
            MM(e, p_h[:, 0:256], onesf[:], osrow[:], True, True)))
        cphase("vector", "sV", lambda e: (
            e.tensor_copy(qsb2[:], p_o[:]), e.drain(),
            e.tensor_copy(osb2[:], p_h[:, 0:256])))
        for s in range(32):
            dphase("sync", "dI", [(h3s[:], h3d_h[s * 128:(s + 1) * 128, :])])
            def fk3(e, _s=s):
                e.tensor_tensor(scr[:], h3s[:], qsb2[:], ALU.mult)
                e.drain()
                e.tensor_single_scalar(scr[:], scr[:], MAGIC, ALU.add)
                e.drain()
                e.tensor_single_scalar(k3[:, _s * 256:(_s + 1) * 256], scr[:],
                                       MAGIC, ALU.subtract)
            cphase("vector", "sV", fk3)
        for oc in range(8):
            def fw2(e, _oc=oc):
                for s in range(32):
                    MM(e, p_o[:], w2qb[:, s * 1024 + _oc * 128:s * 1024 + _oc * 128 + 128],
                       k3[:, s * 256:(s + 1) * 256], s == 0, s == 31)
            cphase("tensor", "sT", fw2)
            cphase("vector", "sV", (lambda e, _oc=oc:
                e.tensor_tensor(ffn[:, _oc * 256:(_oc + 1) * 256], p_o[:],
                                osb2[:], ALU.mult)))
        for b in range(2):
            def ftf(e, _b=b):
                for oc in range(8):
                    e.transpose(p_x[:, oc * 128:(oc + 1) * 128],
                                ffn[:, oc * 256 + _b * 128:oc * 256 + _b * 128 + 128],
                                idb[:])
            cphase("tensor", "sT", ftf)
            cphase("vector", "sV", lambda e: (
                e.tensor_copy(ftm[:], p_x[:]), e.drain(),
                e.tensor_copy(ftm32[:], ftm[:])))
            r0 = w0 + 1 + b * 128
            dphase("sync", "dI", [(y1b[:], y1s_h[r0:r0 + 128, :])])
            cphase("vector", "sV", lambda e: (
                e.tensor_add(res[:], y1b[:], ftm32[:])))
            def fln(e):
                e.bn_stats(stats[:, 0:6], res[:, 0:512])
                e.bn_stats(stats[:, 6:12], res[:, 512:1024])
                e.drain()
                e.bn_aggr(mv[:], stats[:])
            cphase("vector", "sV", fln)
            cphase("scalar", "sS", lambda e:
                e.activation(sc1[:], mv[:, 1:2], AF.Sqrt, bias=epsb[:]))
            def fln2(e):
                e.reciprocal(sc2[:], sc1[:])
                e.drain()
                e.tensor_single_scalar(sc3[:], mv[:, 0:1], -1.0, ALU.mult)
                e.drain()
                e.tensor_scalar(y1[:], res[:], sc3[:], sc2[:], ALU.add, ALU.mult)
            cphase("vector", "sV", fln2)
            dphase("gpsimd", "dO", [(out_h[r0 - 1:r0 + 127, :], y1[:])])

    def fend(e):
        e.wait_ge(sems["dO"], cnt["dO"])
    prog.append(("gpsimd", [tuple(last)], lambda e: e.wait_ge(sems["dO"], cnt["dO"]), None, 0))

    with cx, nc.Block() as block:
        for engname, meth in [("sync", block.sync), ("tensor", block.tensor),
                              ("vector", block.vector), ("scalar", block.scalar),
                              ("gpsimd", block.gpsimd)]:
            def body(e, _en=engname):
                for (en, waits, fn, sn, n) in prog:
                    if en != _en:
                        continue
                    for (ws, wv) in waits:
                        e.wait_ge(sems[ws], wv)
                    fn(e)
            meth(body)
    return nc


_NC_CACHE = []

def kernel(L0_attn_x, L0_attn_Wq, L0_attn_bq, L0_attn_Wdown, L0_attn_bdown,
           L0_attn_Wup, L0_attn_bup, L0_attn_Wout, L0_attn_bout,
           L0_ffn_w1, L0_ffn_conv_w, L0_ffn_conv_b, L0_ffn_w2):
    f32 = np.float32
    bf = ml_dtypes.bfloat16
    x = np.asarray(L0_attn_x, f32).reshape(16384, 1024)
    Wup = np.asarray(L0_attn_Wup, f32)
    wuk = np.concatenate([Wup[:, h * 128:h * 128 + 64] for h in range(16)], axis=1)
    wuv = np.concatenate([Wup[:, h * 128 + 64:h * 128 + 128] for h in range(16)], axis=1)
    bup = np.asarray(L0_attn_bup, f32)
    buk = np.concatenate([bup[h * 128:h * 128 + 64] for h in range(16)])[None, :]
    buv = np.concatenate([bup[h * 128 + 64:h * 128 + 128] for h in range(16)])[None, :]
    convw = np.asarray(L0_ffn_conv_w, f32)
    convb = np.asarray(L0_ffn_conv_b, f32)
    cwm = np.zeros((128, 128), f32)
    for tap in range(3):
        cwm[:, tap * 32:(tap + 1) * 32] = convw[:, 0, tap].reshape(32, 128).T
    cwm[:, 96:128] = convb.reshape(32, 128).T
    mq = np.zeros((8, 2048), f32)
    mk = np.zeros((8, 2048), f32)
    col = np.arange(128)
    for j in range(8):
        qpat = np.where((col % 8) != j, BIG, 0.0)
        kpat = np.where((col // 16) == j, -BIG, 0.0)
        mq[j, :] = np.tile(qpat, 16)
        mk[j, :] = np.tile(kpat, 16)
    shared = {
        "wq": np.asarray(L0_attn_Wq, f32), "wdn": np.asarray(L0_attn_Wdown, f32),
        "wuk": np.ascontiguousarray(wuk), "wuv": np.ascontiguousarray(wuv),
        "wo": np.asarray(L0_attn_Wout, f32),
        "w1t": np.ascontiguousarray(np.asarray(L0_ffn_w1, f32).T),
        "w2t": np.ascontiguousarray(np.asarray(L0_ffn_w2, f32).T),
        "bq": np.asarray(L0_attn_bq, f32)[None, :],
        "bdn": np.asarray(L0_attn_bdown, f32)[None, :],
        "buk": np.ascontiguousarray(buk), "buv": np.ascontiguousarray(buv),
        "bo": np.asarray(L0_attn_bout, f32)[None, :],
        "cw": cwm,
        "mq": mq.astype(bf), "mk": mk.astype(bf),
        "idb": np.eye(128, dtype=bf), "idf": np.eye(128, dtype=f32),
        "onesb": np.ones((1, 512), bf), "onesf": np.ones((1, 128), f32),
        "onescf": np.ones((128, 1), f32),
    }
    in_maps = []
    for c in range(NCORE):
        g0 = c * CT - 1
        rows = np.clip(np.arange(g0, g0 + ST), 0, 16383)
        xc = x[rows]
        hmv = np.ones(ST, f32)
        hmv[2050:] = 0.0
        if c in (0, 4):
            hmv[0] = 0.0
        if c in (3, 7):
            hmv[2049] = 0.0
        hm2 = np.ascontiguousarray(
            np.pad(hmv, (0, 128 * NT1 - ST)).reshape(NT1, 128).T)
        m = dict(shared)
        m["xfm"] = np.ascontiguousarray(xc.T)
        m["xtm"] = np.ascontiguousarray(xc)
        m["hm"] = np.ascontiguousarray(hm2)
        in_maps.append(m)
    if not _NC_CACHE:
        _NC_CACHE.append(build_nc())
    nc = _NC_CACHE[0]
    res = run_bass_kernel_spmd(nc, in_maps, core_ids=list(range(NCORE)))
    out = np.concatenate([r["out"] for r in res.results], axis=0)
    return out.reshape(2, 8192, 1024).astype(np.float32)
